# version 02
# baseline (speedup 1.0000x reference)
import numpy as np
import ml_dtypes
import concourse.bass as bass
import concourse.mybir as mybir
from concourse.bass_utils import run_bass_kernel_spmd

F32 = mybir.dt.float32
BF16 = mybir.dt.bfloat16
AF = mybir.ActivationFunctionType
ALU = mybir.AluOpType
AX = mybir.AxisListType
BF = ml_dtypes.bfloat16

D = 1024
S = 8192
B = 2
NCORE = 8
T = 2048
NT = T // 128
NC = D // 128
LN_EPS = 1e-5

_BUILD_CACHE = {}


def _cached(name, fn):
    if name not in _BUILD_CACHE:
        _BUILD_CACHE[name] = fn()
    return _BUILD_CACHE[name]


def _ident128():
    return np.eye(128, dtype=BF)


ENGINES = ("sync", "gpsimd", "vector", "scalar", "tensor")
DMA_QUEUES = ("sync", "gpsimd", "scalar", "vector")


class Sched:
    def __init__(self, nc):
        self.nc = nc
        self.ops = []
        self.seq = {e: 0 for e in ENGINES}
        self.dma_seq = {q: 0 for q in DMA_QUEUES}
        self.writer = {}
        self.readers = {}
        self.hwm = {e: {} for e in ENGINES}
        self.psum_keys = set()

    def _dep_token(self, dep):
        kind, src, val = dep
        return (kind, src), val

    def _add_waits(self, engine, deps, waits):
        for dep in deps:
            kind, src, val = dep
            if kind == "E" and src == engine and engine == "tensor":
                continue
            if kind == "D":
                val = self.dma_seq[src]
            tok = (kind, src)
            if self.hwm[engine].get(tok, -1) >= val:
                continue
            self.hwm[engine][tok] = val
            waits.append((tok, val))

    def op(self, engine, fn, reads=(), writes=(), dma=False):
        reads = list(reads)
        writes = list(writes)
        for k in list(reads):
            if k in self.psum_keys:
                reads.remove(k)
                writes.append(k)
        waits = []
        deps = []
        for k in reads:
            w = self.writer.get(k)
            if w:
                deps.append(w)
        for k in writes:
            w = self.writer.get(k)
            if w:
                deps.append(w)
            deps.extend(self.readers.get(k, ()))
        self._add_waits(engine, deps, waits)
        if dma:
            self.dma_seq[engine] += 16
            me = ("D", engine, self.dma_seq[engine])
        else:
            self.seq[engine] += 1
            me = ("E", engine, self.seq[engine])
        for k in reads:
            self.readers.setdefault(k, []).append(me)
        for k in writes:
            self.writer[k] = me
            self.readers[k] = []
        self.ops.append((engine, fn, waits, me))
        return me

    def dma(self, queue, out, in_, reads=(), writes=(), **kw):
        return self.op(queue, lambda e: e.dma_start(out=out, in_=in_, **kw),
                       reads=reads, writes=writes, dma=True)

    def finalize(self, block, sems):
        per_engine = {e: [] for e in ENGINES}
        for engine, fn, waits, me in self.ops:
            per_engine[engine].append((fn, waits, me))
        eng_attr = {"sync": block.sync, "gpsimd": block.gpsimd,
                    "vector": block.vector, "scalar": block.scalar,
                    "tensor": block.tensor}
        for engine in ENGINES:
            ops = per_engine[engine]
            if not ops:
                continue

            def make(ops):
                def body(eng):
                    for fn, waits, me in ops:
                        for tok, val in waits:
                            eng.wait_ge(sems[tok], val)
                        inst = fn(eng)
                        kind, src, val = me
                        if kind == "D":
                            inst.then_inc(sems[("D", src)], 16)
                        else:
                            inst.then_inc(sems[("E", src)], 1)
                return body
            eng_attr[engine](make(ops))

    def finish_waits(self, engine="gpsimd"):
        out = []
        for q in DMA_QUEUES:
            if self.dma_seq[q]:
                out.append((("D", q), self.dma_seq[q]))
        for e2 in ENGINES:
            if e2 != engine and self.seq[e2]:
                out.append((("E", e2), self.seq[e2]))
        return out


def build_sched(nc):
    return Sched(nc)


def run_sched(nc, sched, ctx):
    sems = {}
    for e in ENGINES:
        if sched.seq[e]:
            sems[("E", e)] = ctx.enter_context(nc.semaphore(f"sem_e_{e}"))
        if e in DMA_QUEUES and sched.dma_seq[e]:
            sems[("D", e)] = ctx.enter_context(nc.semaphore(f"sem_d_{e}"))
    settle_sb = ctx.enter_context(nc.sbuf_tensor("settle_sb", [1, 16], F32))
    _delay = {
        "vector": lambda eng: eng.memset(settle_sb[0:1, 0:4], 0.0),
        "scalar": lambda eng: eng.copy(settle_sb[0:1, 4:8], settle_sb[0:1, 8:12]),
        "gpsimd": lambda eng: eng.memset(settle_sb[0:1, 12:16], 0.0),
    }
    block = ctx.enter_context(nc.Block())
    term_eng = "gpsimd"
    tw = sched.finish_waits(term_eng)

    def term(eng):
        for tok, val in tw[:-1]:
            eng.wait_ge(sems[tok], val)
        tok, val = tw[-1]
        return eng.wait_ge(sems[tok], val)
    per = sched
    per.ops.append((term_eng, None, [], None))

    per_engine = {e: [] for e in ENGINES}
    for item in per.ops:
        engine = item[0]
        per_engine[engine].append(item)
    eng_attr = {"sync": block.sync, "gpsimd": block.gpsimd,
                "vector": block.vector, "scalar": block.scalar,
                "tensor": block.tensor}
    for engine in ENGINES:
        ops = per_engine[engine]
        if not ops:
            continue

        def make(ops, engine):
            def body(eng):
                for item in ops:
                    if item[1] is None:
                        for tok, val in tw:
                            eng.wait_ge(sems[tok], val)
                        continue
                    _, fn, waits, me = item
                    need_delay = False
                    for tok, val in waits:
                        eng.wait_ge(sems[tok], val)
                        if tok[0] == "E" and tok[1] in ("vector", "scalar", "gpsimd"):
                            need_delay = True
                    if need_delay and engine in _delay:
                        _delay[engine](eng)
                    inst = fn(eng)
                    kind, src, val = me
                    if kind == "D":
                        inst.then_inc(sems[("D", src)], 16)
                    else:
                        inst.then_inc(sems[("E", src)], 1)
            return body
        eng_attr[engine](make(ops, engine))


def emit_ln_tile(sc, xtile, key_x, yout, key_y, stat, key_stat, sq, key_sq, epsc=None, p=128):
    sc.op("vector", lambda v: v.reduce_sum(stat[:, 0:1], xtile, axis=AX.X),
          reads=[key_x], writes=[key_stat])
    sc.op("vector", lambda v: v.tensor_scalar(stat[:, 1:2], stat[:, 0:1], 1.0 / D, None, ALU.mult),
          reads=[key_stat], writes=[key_stat])
    sc.op("scalar", lambda s: s.activation(sq[:, :], xtile, AF.Square),
          reads=[key_x], writes=[key_sq])
    sc.op("vector", lambda v: v.reduce_sum(stat[:, 2:3], sq[0:p, :], axis=AX.X),
          reads=[key_sq], writes=[key_stat + "s2"])
    sc.op("vector", lambda v: v.tensor_scalar(stat[:, 3:4], stat[:, 2:3], 1.0 / D, None, ALU.mult),
          reads=[key_stat + "s2"], writes=[key_stat])
    sc.op("vector", lambda v: v.tensor_tensor(stat[:, 4:5], stat[:, 1:2], stat[:, 1:2], ALU.mult),
          reads=[key_stat], writes=[key_stat])
    sc.op("vector", lambda v: v.tensor_sub(stat[:, 4:5], stat[:, 3:4], stat[:, 4:5]),
          reads=[key_stat], writes=[key_stat])
    sc.op("scalar", lambda s: s.activation(stat[:, 5:6], stat[:, 4:5], AF.Sqrt, bias=epsc[0:p, 0:1]),
          reads=[key_stat, "epsc"], writes=[key_stat + "sd"])
    sc.op("vector", lambda v: v.reciprocal(stat[:, 6:7], stat[:, 5:6]),
          reads=[key_stat + "sd"], writes=[key_stat + "rs"])
    sc.op("vector", lambda v: v.tensor_scalar(yout, xtile, stat[:, 1:2], stat[:, 6:7],
                                              ALU.subtract, ALU.mult),
          reads=[key_x, key_stat, key_stat + "rs"], writes=[key_y])


FFN0_DEBUG = False

def build_ffn0():
    nc = bass.Bass("TRN2", target_bir_lowering=False)
    if FFN0_DEBUG:
        dbg = {n: nc.dram_tensor(n, [128, NC * (T + 4)], BF16, kind="ExternalOutput").ap()
               for n in ("d_ynT", "d_xrT", "d_xkT", "d_rT", "d_kact")}
    xloc = nc.dram_tensor("xloc", [T, D], F32, kind="ExternalInput").ap()
    xprev = nc.dram_tensor("xprev", [1, D], F32, kind="ExternalInput").ap()
    wrt = nc.dram_tensor("wrt", [D, D], F32, kind="ExternalInput")
    wkt = nc.dram_tensor("wkt", [D, D], F32, kind="ExternalInput")
    wvt = nc.dram_tensor("wvt", [D, D], F32, kind="ExternalInput")
    mur = nc.dram_tensor("mur", [D], F32, kind="ExternalInput")
    muk = nc.dram_tensor("muk", [D], F32, kind="ExternalInput")
    ident = nc.dram_tensor("ident", [128, 128], BF16, kind="ExternalInput").ap()
    y = nc.dram_tensor("y", [T, D], F32, kind="ExternalOutput").ap()

    from contextlib import ExitStack
    with ExitStack() as ctx:
        e = ctx.enter_context
        ynT = e(nc.sbuf_tensor("ynT", [128, NC, T + 4], BF16))
        xrT = e(nc.sbuf_tensor("xrT", [128, NC, T], BF16))
        xkT = e(nc.sbuf_tensor("xkT", [128, NC, T], BF16))
        wR = e(nc.sbuf_tensor("wR", [128, NC, D], BF16))
        wK = e(nc.sbuf_tensor("wK", [128, NC, D], BF16))
        rT = ynT
        wV = wR
        idt = e(nc.sbuf_tensor("idt", [128, 128], BF16))
        murT = e(nc.sbuf_tensor("murT", [128, NC], F32))
        mukT = e(nc.sbuf_tensor("mukT", [128, NC], F32))
        xt = [e(nc.sbuf_tensor(f"xt{i}", [128, D], F32)) for i in range(2)]
        sq = [e(nc.sbuf_tensor("sq0", [128, D], F32))] * 2
        yln = [e(nc.sbuf_tensor(f"yln{i}", [128, D], BF16)) for i in range(2)]
        stat = [e(nc.sbuf_tensor(f"stat{i}", [128, 8], F32)) for i in range(2)]
        xh = e(nc.sbuf_tensor("xh", [1, D], F32))
        sqh = e(nc.sbuf_tensor("sqh", [1, D], F32))
        stath = e(nc.sbuf_tensor("stath", [1, 8], F32))
        epsc = e(nc.sbuf_tensor("epsc", [128, 1], F32))
        yprev = e(nc.sbuf_tensor("yprev", [1, D], BF16))
        dT = e(nc.sbuf_tensor("dT", [128, T], BF16))
        tk = e(nc.sbuf_tensor("tk", [128, T], BF16))
        gT = [e(nc.sbuf_tensor(f"gT{i}", [128, 512], BF16)) for i in range(2)]
        krelu = [e(nc.sbuf_tensor(f"krelu{i}", [128, 512], BF16)) for i in range(2)]
        ystage = [e(nc.sbuf_tensor(f"ystage{i}", [128, D], F32)) for i in range(2)]
        xt2 = [e(nc.sbuf_tensor(f"xt2{i}", [128, D], F32)) for i in range(4)]

        ptrA = e(nc.psum_tensor("ptrA", [128, 1024], BF16))
        phalo = e(nc.psum_tensor("phalo", [128, 8], F32))
        prj = [e(nc.psum_tensor(f"prj{i}", [128, 512], F32)) for i in range(2)]
        ptok = [e(nc.psum_tensor(f"ptok{i}", [128, 1024], BF16)) for i in range(4)]

        sc = Sched(nc)
        sc.psum_keys |= {"ptrA", "phalo", "prj0", "prj1", "ptok0", "ptok1", "ptok2", "ptok3"}

        sc.op("vector", lambda v: v.memset(epsc[:, :], LN_EPS), writes=["epsc"])
        sc.dma("sync", idt[:, :], ident, writes=["idt"])
        sc.dma("sync", murT[:, :], bass.AP(mur, 0, [[1, 128], [128, NC]]), writes=["murT"],
               allow_slow_non_contiguous=True)
        sc.dma("sync", mukT[:, :], bass.AP(muk, 0, [[1, 128], [128, NC]]), writes=["mukT"],
               allow_slow_non_contiguous=True)
        sc.dma("sync", xh[:, :], xprev, writes=["xh"])
        for wt, buf, k in ((wrt, wR, "wR"), (wkt, wK, "wK")):
            sc.dma("gpsimd", buf[:, :, :],
                   bass.AP(wt, 0, [[D, 128], [128 * D, NC], [1, D]]), writes=[k])

        emit_ln_tile(sc, xh[:, :], "xh", yprev[:, :], "yprev", stath, "stath", sqh, "sqh", epsc=epsc, p=1)
        for c in range(NC):
            sc.op("tensor", (lambda c: lambda t: t.matmul(
                phalo[:, c:c + 1], yprev[0:1, c * 128:(c + 1) * 128],
                idt[0:1, 0:1], start=True, stop=True))(c),
                reads=["yprev", "idt"], writes=["phalo"])
        sc.op("vector", lambda v: v.tensor_copy(ynT[:, :, 0], phalo[:, 0:NC]),
              reads=["phalo"], writes=["ynT"])

        for i in range(NT):
            bi = i % 2
            sc.dma("sync", xt[bi][:, :], xloc[i * 128:(i + 1) * 128, :], writes=[f"xt{bi}"])
            emit_ln_tile(sc, xt[bi][:, :], f"xt{bi}", yln[bi][:, :], f"yln{bi}",
                         stat[bi], f"stat{bi}", sq[bi], f"sq{bi}", epsc=epsc)
            for c in range(NC):
                sc.op("tensor", (lambda bi, c: lambda t: t.transpose(
                    ptrA[:, c * 128:(c + 1) * 128], yln[bi][:, c * 128:(c + 1) * 128], idt[:, :]))(bi, c),
                    reads=[f"yln{bi}", "idt"], writes=["ptrA"])
            sc.op("vector", (lambda i: lambda v: v.tensor_copy(
                ynT[:, :, 1 + i * 128:1 + (i + 1) * 128],
                ptrA[:, :].rearrange("p (c f) -> p c f", c=NC)))(i),
                reads=["ptrA"], writes=["ynT"])

        for c in range(NC):
            sc.op("vector", (lambda c: lambda v: v.tensor_sub(
                dT[:, :], ynT[:, c, 1:T + 1], ynT[:, c, 0:T]))(c),
                reads=["ynT"], writes=["dT"])
            sc.op("vector", (lambda c: lambda v: v.scalar_tensor_tensor(
                xrT[:, c, :], dT[:, :], murT[:, c:c + 1], ynT[:, c, 0:T], ALU.mult, ALU.add))(c),
                reads=["dT", "murT", "ynT"], writes=[("xrT", c)])
            sc.op("gpsimd", (lambda c: lambda g: g.tensor_scalar(
                tk[:, :], dT[:, :], mukT[:, c:c + 1], None, ALU.mult))(c),
                reads=["dT", "mukT"], writes=["tk"])
            sc.op("gpsimd", (lambda c: lambda g: g.tensor_add(
                xkT[:, c, :], tk[:, :], ynT[:, c, 0:T]))(c),
                reads=["tk", "ynT"], writes=[("xkT", c)])

        if FFN0_DEBUG:
            sc.dma("gpsimd", dbg["d_ynT"][:, 0:NC * (T + 4)].rearrange("p (c t) -> p c t", c=NC),
                   ynT[:, :, :], reads=["ynT"])
            sc.dma("gpsimd", dbg["d_xrT"][:, 0:NC * T].rearrange("p (c t) -> p c t", c=NC),
                   xrT[:, :, :],
                   reads=[("xrT", c) for c in range(NC)])
            sc.dma("gpsimd", dbg["d_xkT"][:, 0:NC * T].rearrange("p (c t) -> p c t", c=NC),
                   xkT[:, :, :],
                   reads=[("xkT", c) for c in range(NC)])

        for d in range(NC):
            for tt in range(4):
                ps = prj[tt % 2]
                pk = f"prj{tt % 2}"
                for cc in range(NC):
                    sc.op("tensor", (lambda d, tt, cc, ps: lambda t: t.matmul(
                        ps[:, :], wR[:, cc, d * 128:(d + 1) * 128],
                        xrT[:, cc, tt * 512:(tt + 1) * 512],
                        start=(cc == 0), stop=(cc == NC - 1)))(d, tt, cc, ps),
                        reads=["wR", ("xrT", cc)], writes=[pk])
                sc.op("scalar", (lambda d, tt, ps: lambda s: s.activation(
                    rT[:, d, tt * 512:(tt + 1) * 512], ps[:, :], AF.Sigmoid))(d, tt, ps),
                    reads=[pk], writes=[("rT", d), "ynT"])

        if FFN0_DEBUG:
            sc.dma("gpsimd", dbg["d_rT"][:, 0:NC * T].rearrange("p (c t) -> p c t", c=NC),
                   ynT[:, :, 0:T],
                   reads=[("rT", d) for d in range(NC)])

        sc.dma("gpsimd", wV[:, :, :],
               bass.AP(wvt, 0, [[D, 128], [128 * D, NC], [1, D]]), writes=["wV", "wR"])

        for d in range(NC):
            for tt in range(4):
                ps = prj[tt % 2]
                pk = f"prj{tt % 2}"
                for cc in range(NC):
                    sc.op("tensor", (lambda d, tt, cc, ps: lambda t: t.matmul(
                        ps[:, :], wK[:, cc, d * 128:(d + 1) * 128],
                        xkT[:, cc, tt * 512:(tt + 1) * 512],
                        start=(cc == 0), stop=(cc == NC - 1)))(d, tt, cc, ps),
                        reads=["wK", ("xkT", cc)], writes=[pk])
                kb = krelu[tt % 2]
                sc.op("vector", (lambda ps, kb: lambda v: v.tensor_relu(kb[:, :], ps[:, :]))(ps, kb),
                      reads=[pk], writes=[f"krelu{tt % 2}"])
                sc.op("scalar", (lambda d, tt, kb: lambda s: s.activation(
                    xrT[:, d, tt * 512:(tt + 1) * 512], kb[:, :], AF.Square))(d, tt, kb),
                    reads=[f"krelu{tt % 2}"], writes=[("xrT", d)])

        if FFN0_DEBUG:
            sc.dma("gpsimd", dbg["d_kact"][:, 0:NC * T].rearrange("p (c t) -> p c t", c=NC),
                   xrT[:, :, :],
                   reads=[("xrT", c) for c in range(NC)])

        for tt in range(4):
            for j in range(4):
                g = tt * 4 + j
                sc.dma("sync", xt2[j][:, :], xloc[g * 128:(g + 1) * 128, :], writes=[f"xt2{j}"])
            for d in range(NC):
                ps = prj[d % 2]
                pk = f"prj{d % 2}"
                for cc in range(NC):
                    sc.op("tensor", (lambda d, tt, cc, ps: lambda t: t.matmul(
                        ps[:, :], wV[:, cc, d * 128:(d + 1) * 128],
                        xrT[:, cc, tt * 512:(tt + 1) * 512],
                        start=(cc == 0), stop=(cc == NC - 1)))(d, tt, cc, ps),
                        reads=["wV", ("xrT", cc)], writes=[pk])
                gb = gT[d % 2]
                sc.op("vector", (lambda d, tt, ps, gb: lambda v: v.tensor_tensor(
                    gb[:, :], ps[:, :], rT[:, d, tt * 512:(tt + 1) * 512], ALU.mult))(d, tt, ps, gb),
                    reads=[pk, ("rT", d)], writes=[f"gT{d % 2}"])
                for j in range(4):
                    sc.op("tensor", (lambda d, j, gb: lambda t: t.transpose(
                        ptok[j][:, d * 128:(d + 1) * 128], gb[:, j * 128:(j + 1) * 128], idt[:, :]))(d, j, gb),
                        reads=[f"gT{d % 2}", "idt"], writes=[f"ptok{j}"])
            for j in range(4):
                yb = ystage[j % 2]
                sc.op("vector", (lambda j, yb: lambda v: v.scalar_tensor_tensor(
                    yb[:, :], ptok[j][:, :], 0.5, xt2[j][:, :], ALU.mult, ALU.add))(j, yb),
                    reads=[f"ptok{j}", f"xt2{j}"], writes=[f"ystage{j % 2}"])
                g = tt * 4 + j
                sc.dma("scalar", y[g * 128:(g + 1) * 128, :], yb[:, :], reads=[f"ystage{j % 2}"])

        run_sched(nc, sc, ctx)
    return nc


def _halo_rows(x2d):
    rows = []
    for c in range(NCORE):
        if (c * T) % S == 0:
            rows.append(np.zeros((1, D), np.float32))
        else:
            rows.append(np.ascontiguousarray(x2d[c * T - 1:c * T]))
    return rows


def launch_ffn0(x2d, w):
    nc = _cached("ffn0", build_ffn0)
    wrt = np.ascontiguousarray(w["W_r"].T)
    wkt = np.ascontiguousarray(w["W_k"].T)
    wvt = np.ascontiguousarray(w["W_v"].T)
    ident = _cached("ident", _ident128)
    halos = _halo_rows(x2d)
    in_maps = []
    for c in range(NCORE):
        in_maps.append({
            "xloc": np.ascontiguousarray(x2d[c * T:(c + 1) * T]),
            "xprev": halos[c],
            "wrt": wrt, "wkt": wkt, "wvt": wvt,
            "mur": w["mu_r"], "muk": w["mu_k"],
            "ident": ident,
        })
    res = run_bass_kernel_spmd(nc, in_maps, core_ids=list(range(NCORE)))
    return np.concatenate([r["y"] for r in res.results], axis=0)




def launch_attn0(x2d, w):
    nc = _cached("attn0", build_attn0)
    ident = _cached("ident", _ident128)
    identf = _cached("identf", lambda: np.eye(128, dtype=np.float32))
    onesr = _cached("onesr", lambda: np.ones((1, 128), dtype=BF))
    in_maps = []
    for c in range(NCORE):
        b = c // 4
        hq = 2 * (c % 4)
        in_maps.append({
            "xbatch": np.ascontiguousarray(x2d[b * S:(b + 1) * S]),
            "xloc": np.ascontiguousarray(x2d[c * T:(c + 1) * T]),
            "wq": np.ascontiguousarray(w["Wq"][:, hq * 128:(hq + 2) * 128]),
            "wk": np.ascontiguousarray(w["Wk"][:, hq * 128:(hq + 2) * 128]),
            "wv": np.ascontiguousarray(w["Wv"][:, hq * 1024:(hq + 2) * 1024]),
            "bq": np.ascontiguousarray(w["bq"][hq * 128:(hq + 2) * 128]),
            "bk": np.ascontiguousarray(w["bk"][hq * 128:(hq + 2) * 128]),
            "bv": np.ascontiguousarray(w["bv"][hq * 1024:(hq + 2) * 1024]),
            "gamma": w["gamma"], "beta": w["beta"],
            "ident": ident, "identf": identf, "onesr": onesr,
        })
    res = run_bass_kernel_spmd(nc, in_maps, core_ids=list(range(NCORE)))
    return np.concatenate([r["y"] for r in res.results], axis=0)


NEXP = 16
NGRP = 2
GNT = NT // NGRP


def build_moe():
    nc = bass.Bass("TRN2", target_bir_lowering=False)
    xloc = nc.dram_tensor("xloc", [T, D], F32, kind="ExternalInput").ap()
    wr = nc.dram_tensor("wr", [D, NEXP], F32, kind="ExternalInput")
    br = nc.dram_tensor("br", [NEXP], F32, kind="ExternalInput")
    w1 = nc.dram_tensor("w1", [NEXP, D, D], F32, kind="ExternalInput")
    b1 = nc.dram_tensor("b1", [NEXP, D], F32, kind="ExternalInput")
    g1 = nc.dram_tensor("g1", [NEXP, D], F32, kind="ExternalInput")
    be1 = nc.dram_tensor("be1", [NEXP, D], F32, kind="ExternalInput")
    w2 = nc.dram_tensor("w2", [NEXP, D, D], F32, kind="ExternalInput")
    b2 = nc.dram_tensor("b2", [NEXP, D], F32, kind="ExternalInput")
    ident = nc.dram_tensor("ident", [128, 128], BF16, kind="ExternalInput").ap()
    onesr = nc.dram_tensor("onesr", [1, 128], BF16, kind="ExternalInput").ap()
    identf = nc.dram_tensor("identf", [128, 128], F32, kind="ExternalInput").ap()
    y = nc.dram_tensor("y", [T, D], F32, kind="ExternalOutput").ap()

    from contextlib import ExitStack
    with ExitStack() as ctx:
        e = ctx.enter_context
        idtf = e(nc.sbuf_tensor("idtf", [128, 128], F32))
        xnT = e(nc.sbuf_tensor("xnT", [128, NC, T], BF16))
        ysb = e(nc.sbuf_tensor("ysb", [128, GNT, D], F32))
        wA = [e(nc.sbuf_tensor(f"wA{i}", [128, NC, D], BF16)) for i in range(2)]
        wB = [e(nc.sbuf_tensor(f"wB{i}", [128, NC, D], BF16)) for i in range(2)]
        g1rep = [e(nc.sbuf_tensor(f"g1rep{i}", [128, D], BF16)) for i in range(2)]
        be1rep = [e(nc.sbuf_tensor(f"be1rep{i}", [128, D], BF16)) for i in range(2)]
        b1row = [e(nc.sbuf_tensor(f"b1row{i}", [1, D], BF16)) for i in range(2)]
        b2sb = e(nc.sbuf_tensor("b2sb", [16, D], BF16))
        wrsb = e(nc.sbuf_tensor("wrsb", [128, NC, NEXP], BF16))
        brrep = e(nc.sbuf_tensor("brrep", [128, NEXP], F32))
        idt = e(nc.sbuf_tensor("idt", [128, 128], BF16))
        ones = e(nc.sbuf_tensor("ones", [1, 128], BF16))
        epsc = e(nc.sbuf_tensor("epsc", [128, 1], F32))
        xt = [e(nc.sbuf_tensor(f"xt{i}", [128, D], F32)) for i in range(2)]
        sq = e(nc.sbuf_tensor("sq0", [128, D], F32))
        yln = [e(nc.sbuf_tensor(f"yln{i}", [128, D], BF16)) for i in range(2)]
        stat = [e(nc.sbuf_tensor(f"stat{i}", [128, 8], F32)) for i in range(2)]
        lsb = e(nc.sbuf_tensor("lsb", [128, NEXP], F32))
        lsb2 = e(nc.sbuf_tensor("lsb2", [128, NEXP], F32))
        usb = e(nc.sbuf_tensor("usb", [128, NEXP], F32))
        m1sb = e(nc.sbuf_tensor("m1sb", [128, 4], F32))
        masks = e(nc.sbuf_tensor("masks", [128, NEXP], F32))
        wsel = e(nc.sbuf_tensor("wsel", [128, NT, NEXP], F32))
        wrow = e(nc.sbuf_tensor("wrow", [16, NT, 128], BF16))
        wselb = e(nc.sbuf_tensor("wselb", [128, NEXP], BF16))
        t2b = [e(nc.sbuf_tensor(f"t2b{i}", [128, D], BF16)) for i in range(2)]
        hact = [e(nc.sbuf_tensor(f"hact{i}", [128, D], BF16)) for i in range(2)]
        hwT = [e(nc.sbuf_tensor(f"hwT{i}", [128, NC, 128], BF16)) for i in range(2)]
        estat = [e(nc.sbuf_tensor(f"estat{i}", [128, 8], F32)) for i in range(2)]
        ystage = [e(nc.sbuf_tensor(f"ystage{i}", [128, D], F32)) for i in range(2)]

        ptrA = e(nc.psum_tensor("ptrA", [128, 1024], BF16))
        hps = [e(nc.psum_tensor(f"hps{i}", [128, 1024], F32)) for i in range(2)]
        yps = e(nc.psum_tensor("yps", [128, 1024], F32))
        prt = e(nc.psum_tensor("prt", [128, 144], F32))

        sc = Sched(nc)
        sc.psum_keys |= {"ptrA", "hps0", "hps1", "yps", "prt"}

        sc.op("vector", lambda v: v.memset(epsc[:, :], LN_EPS), writes=["epsc"])
        sc.dma("sync", idt[:, :], ident, writes=["idt"])
        sc.dma("sync", ones[:, :], onesr, writes=["ones"])
        sc.dma("sync", idtf[:, :], identf, writes=["idtf"])
        sc.dma("gpsimd", wrsb[:, :, :], bass.AP(wr, 0, [[NEXP, 128], [128 * NEXP, NC], [1, NEXP]]), writes=["wrsb"])
        sc.dma("sync", brrep[:, :], bass.AP(br, 0, [[0, 128], [1, NEXP]]), writes=["brrep"])
        sc.dma("gpsimd", b2sb[:, :], bass.AP(b2, 0, [[D, 16], [1, D]]), writes=["b2sb"])

        for i in range(NT):
            bi = i % 2
            sc.dma("sync", xt[bi][:, :], xloc[i * 128:(i + 1) * 128, :], writes=[f"xt{bi}"])
            emit_ln_tile(sc, xt[bi][:, :], f"xt{bi}", yln[bi][:, :], f"yln{bi}",
                         stat[bi], f"stat{bi}", sq, "sq", epsc=epsc)
            for c in range(NC):
                sc.op("tensor", (lambda bi, c: lambda t: t.transpose(
                    ptrA[:, c * 128:(c + 1) * 128], yln[bi][:, c * 128:(c + 1) * 128], idt[:, :]))(bi, c),
                    reads=[f"yln{bi}", "idt"], writes=["ptrA"])
            sc.op("vector", (lambda i: lambda v: v.tensor_copy(
                xnT[:, :, i * 128:(i + 1) * 128],
                ptrA[:, :].rearrange("p (c f) -> p c f", c=NC)))(i),
                reads=["ptrA"], writes=[("xnT", i)])
            for c in range(NC):
                sc.op("tensor", (lambda i, c: lambda t: t.matmul(
                    prt[:, 0:NEXP], xnT[:, c, i * 128:(i + 1) * 128], wrsb[:, c, :],
                    start=(c == 0), stop=(c == NC - 1)))(i, c),
                    reads=[("xnT", i), "wrsb"], writes=["prt"])
            sc.op("vector", lambda v: v.tensor_add(lsb[:, :], prt[:, 0:NEXP], brrep[:, :]),
                  reads=["prt", "brrep"], writes=["lsb"])
            sc.op("vector", lambda v: v.reduce_max(m1sb[:, 0:1], lsb[:, :], axis=AX.X),
                  reads=["lsb"], writes=["m1"])
            sc.op("vector", lambda v: v.tensor_scalar(masks[:, :], lsb[:, :], m1sb[:, 0:1], None, ALU.is_equal),
                  reads=["lsb", "m1"], writes=["masks"])
            sc.op("vector", lambda v: v.scalar_tensor_tensor(lsb2[:, :], masks[:, :], -1e30, lsb[:, :], ALU.mult, ALU.add),
                  reads=["masks", "lsb"], writes=["lsb2"])
            sc.op("vector", lambda v: v.reduce_max(m1sb[:, 1:2], lsb2[:, :], axis=AX.X),
                  reads=["lsb2"], writes=["m2"])
            sc.op("vector", lambda v: v.tensor_scalar(lsb2[:, :], lsb2[:, :], m1sb[:, 1:2], None, ALU.is_equal),
                  reads=["lsb2", "m2"], writes=["lsb2"])
            sc.op("vector", lambda v: v.tensor_add(masks[:, :], masks[:, :], lsb2[:, :]),
                  reads=["masks", "lsb2"], writes=["masks"])
            sc.op("vector", lambda v: v.tensor_scalar(m1sb[:, 2:3], m1sb[:, 0:1], -1.0, None, ALU.mult),
                  reads=["m1"], writes=["negm1"])
            sc.op("scalar", lambda s: s.activation(usb[:, :], lsb[:, :], AF.Exp, bias=m1sb[:, 2:3]),
                  reads=["lsb", "negm1"], writes=["usb"])
            sc.op("vector", lambda v: v.tensor_mul(usb[:, :], usb[:, :], masks[:, :]),
                  reads=["usb", "masks"], writes=["usb"])
            sc.op("vector", lambda v: v.reduce_sum(m1sb[:, 3:4], usb[:, :], axis=AX.X),
                  reads=["usb"], writes=["den"])
            sc.op("vector", lambda v: v.reciprocal(m1sb[:, 2:3], m1sb[:, 3:4]),
                  reads=["den"], writes=["dinv"])
            sc.op("vector", (lambda i: lambda v: v.tensor_scalar(
                wsel[:, i, :], usb[:, :], m1sb[:, 2:3], None, ALU.mult))(i),
                reads=["usb", "dinv"], writes=[("wsel", i)])
            sc.op("vector", (lambda i: lambda v: v.tensor_copy(wselb[:, :], wsel[:, i, :]))(i),
                  reads=[("wsel", i)], writes=["wselb"])
            sc.op("tensor", (lambda i: lambda t: t.transpose(
                ptrA[0:16, 0:128], wselb[:, :], idt[:, :]))(i),
                reads=["wselb", "idt"], writes=["ptrA"])
            sc.op("vector", (lambda i: lambda v: v.tensor_copy(
                wrow[0:16, i, :], ptrA[0:16, 0:128]))(i),
                reads=["ptrA"], writes=[("wrow", i)])

        for grp in range(NGRP):
            i0 = grp * GNT
            for j in range(GNT):
                i = i0 + j
                for half in range(2):
                    sc.op("tensor", (lambda i, half: lambda t: t.matmul(
                        yps[:, half * 512:(half + 1) * 512], wrow[0:16, i, :],
                        b2sb[:, half * 512:(half + 1) * 512],
                        start=True, stop=True))(i, half),
                        reads=[("wrow", i), "b2sb"], writes=["yps"])
                sc.op("vector", (lambda j: lambda v: v.tensor_copy(ysb[:, j, :], yps[:, :]))(j),
                      reads=["yps"], writes=[("ysb", j)])
            for ei in range(NEXP):
                eb = ei % 2
                sc.dma("gpsimd", wA[eb][:, :, :],
                       bass.AP(w1, ei * D * D, [[D, 128], [128 * D, NC], [1, D]]),
                       writes=[f"wA{eb}"])
                sc.dma("gpsimd", wB[eb][:, :, :],
                       bass.AP(w2, ei * D * D, [[D, 128], [128 * D, NC], [1, D]]),
                       writes=[f"wB{eb}"])
                sc.dma("gpsimd", g1rep[eb][:, :], bass.AP(g1, ei * D, [[0, 128], [1, D]]),
                       writes=[f"g1rep{eb}"])
                sc.dma("gpsimd", be1rep[eb][:, :], bass.AP(be1, ei * D, [[0, 128], [1, D]]),
                       writes=[f"be1rep{eb}"])
                sc.dma("gpsimd", b1row[eb][:, :], bass.AP(b1, ei * D, [[D, 1], [1, D]]),
                       writes=[f"b1row{eb}"])
                for j in range(GNT):
                    i = i0 + j
                    hb = j % 2
                    hp = hps[hb]
                    hk = f"hps{hb}"
                    for half in range(2):
                        for cc in range(NC):
                            sc.op("tensor", (lambda i, cc, half, hp, eb: lambda t: t.matmul(
                                hp[:, half * 512:(half + 1) * 512],
                                xnT[:, cc, i * 128:(i + 1) * 128],
                                wA[eb][:, cc, half * 512:(half + 1) * 512],
                                start=(cc == 0), stop=False))(i, cc, half, hp, eb),
                                reads=[("xnT", i), f"wA{eb}"], writes=[hk])
                        sc.op("tensor", (lambda half, hp, eb: lambda t: t.matmul(
                            hp[:, half * 512:(half + 1) * 512], ones[0:1, :],
                            b1row[eb][0:1, half * 512:(half + 1) * 512],
                            start=False, stop=True))(half, hp, eb),
                            reads=["ones", f"b1row{eb}"], writes=[hk])
                    es = estat[hb]
                    ek = f"estat{hb}"
                    sc.op("vector", (lambda hp, es: lambda v: v.reduce_sum(es[:, 0:1], hp[:, :], axis=AX.X))(hp, es),
                          reads=[hk], writes=[ek])
                    sc.op("scalar", (lambda hp, es: lambda s: s.activation(
                        sq[:, :], hp[:, :], AF.Square))(hp, es),
                        reads=[hk], writes=["sq"])
                    sc.op("vector", (lambda es: lambda v: v.reduce_sum(
                        es[:, 2:3], sq[:, :], axis=AX.X))(es),
                        reads=["sq"], writes=[ek + "s2"])
                    sc.op("vector", (lambda es: lambda v: v.tensor_scalar(
                        es[:, 1:2], es[:, 0:1], 1.0 / D, None, ALU.mult))(es),
                        reads=[ek], writes=[ek])
                    sc.op("vector", (lambda es: lambda v: v.tensor_tensor(
                        es[:, 4:5], es[:, 1:2], es[:, 1:2], ALU.mult))(es),
                        reads=[ek], writes=[ek])
                    sc.op("vector", (lambda es: lambda v: v.scalar_tensor_tensor(
                        es[:, 5:6], es[:, 2:3], 1.0 / D, es[:, 4:5], ALU.mult, ALU.subtract))(es),
                        reads=[ek, ek + "s2"], writes=[ek])
                    sc.op("scalar", (lambda es: lambda s: s.activation(
                        es[:, 6:7], es[:, 5:6], AF.Sqrt, bias=epsc[:, 0:1]))(es),
                        reads=[ek, "epsc"], writes=[ek + "sd"])
                    sc.op("vector", (lambda es: lambda v: v.reciprocal(es[:, 7:8], es[:, 6:7]))(es),
                          reads=[ek + "sd"], writes=[ek + "rs"])
                    t2 = t2b[hb]
                    sc.op("vector", (lambda hp, es, t2: lambda v: v.tensor_scalar(
                        t2[:, :], hp[:, :], es[:, 1:2], es[:, 7:8], ALU.subtract, ALU.mult))(hp, es, t2),
                        reads=[hk, ek, ek + "rs"], writes=[f"t2b{hb}"])
                    sc.op("gpsimd", (lambda t2, eb: lambda g: g.tensor_mul(
                        t2[:, :], t2[:, :], g1rep[eb][:, :]))(t2, eb),
                        reads=[f"t2b{hb}", f"g1rep{eb}"], writes=[f"t2b{hb}"])
                    sc.op("gpsimd", (lambda t2, eb: lambda g: g.tensor_add(
                        t2[:, :], t2[:, :], be1rep[eb][:, :]))(t2, eb),
                        reads=[f"t2b{hb}", f"be1rep{eb}"], writes=[f"t2b{hb}"])
                    ha = hact[hb]
                    sc.op("scalar", (lambda t2, ha: lambda s: s.activation(
                        ha[:, :], t2[:, :], AF.Gelu))(t2, ha),
                        reads=[f"t2b{hb}"], writes=[f"hact{hb}"])
                    sc.op("vector", (lambda ha, i, ei: lambda v: v.tensor_scalar_mul(
                        ha[:, :], ha[:, :], wsel[:, i, ei:ei + 1]))(ha, i, ei),
                        reads=[f"hact{hb}", ("wsel", i)], writes=[f"hact{hb}"])
                    for c in range(NC):
                        sc.op("tensor", (lambda ha, c: lambda t: t.transpose(
                            ptrA[:, c * 128:(c + 1) * 128], ha[:, c * 128:(c + 1) * 128], idt[:, :]))(ha, c),
                            reads=[f"hact{hb}", "idt"], writes=["ptrA"])
                    hw = hwT[hb]
                    sc.op("scalar", (lambda hw: lambda s: s.copy(
                        hw[:, :, :], ptrA[:, :].rearrange("p (c f) -> p c f", c=NC)))(hw),
                        reads=["ptrA"], writes=[f"hwT{hb}"])
                    for half in range(2):
                        for cc in range(NC):
                            sc.op("tensor", (lambda cc, half, hw, eb: lambda t: t.matmul(
                                yps[:, half * 512:(half + 1) * 512], hw[:, cc, :],
                                wB[eb][:, cc, half * 512:(half + 1) * 512],
                                start=(cc == 0), stop=(cc == NC - 1)))(cc, half, hw, eb),
                                reads=[f"hwT{hb}", f"wB{eb}"], writes=["yps"])
                    sc.op("vector", (lambda j: lambda v: v.tensor_add(
                        ysb[:, j, :], ysb[:, j, :], yps[:, :]))(j),
                        reads=["yps", ("ysb", j)], writes=[("ysb", j)])
            for j in range(GNT):
                i = i0 + j
                bi = j % 2
                sc.dma("sync", xt[bi][:, :], xloc[i * 128:(i + 1) * 128, :], writes=[f"xt{bi}"])
                sc.op("vector", (lambda j, bi: lambda v: v.scalar_tensor_tensor(
                    ystage[bi][:, :], ysb[:, j, :], 0.5, xt[bi][:, :], ALU.mult, ALU.add))(j, bi),
                    reads=[("ysb", j), f"xt{bi}"], writes=[f"ystage{bi}"])
                sc.dma("scalar", y[i * 128:(i + 1) * 128, :], ystage[bi][:, :], reads=[f"ystage{bi}"])

        run_sched(nc, sc, ctx)
    return nc


def launch_moe(x2d, w):
    nc = _cached("moe", build_moe)
    ident = _cached("ident", _ident128)
    onesr = _cached("onesr", lambda: np.ones((1, 128), dtype=BF))
    in_maps = []
    for c in range(NCORE):
        in_maps.append({
            "xloc": np.ascontiguousarray(x2d[c * T:(c + 1) * T]),
            "wr": w["Wr"], "br": w["br"], "w1": w["W1"], "b1": w["b1"],
            "g1": w["g1"], "be1": w["be1"], "w2": w["W2"], "b2": w["b2"],
            "ident": ident, "onesr": onesr, "identf": _cached("identf", lambda: np.eye(128, dtype=np.float32)),
        })
    res = run_bass_kernel_spmd(nc, in_maps, core_ids=list(range(NCORE)))
    return np.concatenate([r["y"] for r in res.results], axis=0)




NTB = S // 128
NSTS = S // 512
ATT_SCALE = float(1.0 / np.sqrt(128.0))
SHIFT_MARGIN = 30.0
SUM_FLOOR = 1e-30


def emit_ln2_tile(sc, xtile, key_x, yout, key_y, st, key_st, sq, key_sq, epsc,
                  grep, brep):
    sc.op("vector", lambda v: v.reduce_sum(st[:, 0:1], xtile, axis=AX.X),
          reads=[key_x], writes=[key_st])
    sc.op("scalar", lambda s: s.activation(sq[:, :], xtile, AF.Square),
          reads=[key_x], writes=[key_sq])
    sc.op("vector", lambda v: v.reduce_sum(st[:, 2:3], sq[:, :], axis=AX.X),
          reads=[key_sq], writes=[key_st + "s2"])
    sc.op("vector", lambda v: v.tensor_scalar(st[:, 1:2], st[:, 0:1], 1.0 / D, None, ALU.mult),
          reads=[key_st], writes=[key_st])
    sc.op("vector", lambda v: v.tensor_tensor(st[:, 4:5], st[:, 1:2], st[:, 1:2], ALU.mult),
          reads=[key_st], writes=[key_st])
    sc.op("vector", lambda v: v.scalar_tensor_tensor(st[:, 4:5], st[:, 2:3], 1.0 / D, st[:, 4:5], ALU.mult, ALU.subtract),
          reads=[key_st, key_st + "s2"], writes=[key_st])
    sc.op("scalar", lambda s: s.activation(st[:, 5:6], st[:, 4:5], AF.Sqrt, bias=epsc[:, 0:1]),
          reads=[key_st, "epsc"], writes=[key_st + "sd"])
    sc.op("vector", lambda v: v.reciprocal(st[:, 6:7], st[:, 5:6]),
          reads=[key_st + "sd"], writes=[key_st + "rs"])
    sc.op("vector", lambda v: v.tensor_tensor(st[:, 3:4], st[:, 4:5], st[:, 6:7], ALU.mult),
          reads=[key_st, key_st + "rs"], writes=[key_st + "v2a"])
    sc.op("vector", lambda v: v.tensor_tensor(st[:, 3:4], st[:, 3:4], st[:, 6:7], ALU.mult),
          reads=[key_st + "v2a", key_st + "rs"], writes=[key_st + "v2"])
    sc.op("scalar", lambda s: s.activation(st[:, 7:8], st[:, 3:4], AF.Sqrt, bias=epsc[:, 0:1]),
          reads=[key_st + "v2", "epsc"], writes=[key_st + "sd2"])
    sc.op("vector", lambda v: v.reciprocal(st[:, 7:8], st[:, 7:8]),
          reads=[key_st + "sd2"], writes=[key_st + "r2"])
    sc.op("vector", lambda v: v.tensor_tensor(st[:, 6:7], st[:, 6:7], st[:, 7:8], ALU.mult),
          reads=[key_st + "rs", key_st + "r2"], writes=[key_st + "R"])
    sc.op("vector", lambda v: v.tensor_scalar(sq[:, :], xtile, st[:, 1:2], st[:, 6:7],
                                              ALU.subtract, ALU.mult),
          reads=[key_x, key_st, key_st + "R"], writes=[key_sq])
    sc.op("gpsimd", lambda g: g.tensor_mul(sq[:, :], sq[:, :], grep[:, :]),
          reads=[key_sq, "grep"], writes=[key_sq])
    sc.op("gpsimd", lambda g: g.tensor_add(yout, sq[:, :], brep[:, :]),
          reads=[key_sq, "brep"], writes=[key_y])


ATTN0_DEBUG = False

def build_attn0():
    nc = bass.Bass("TRN2", target_bir_lowering=False)
    if ATTN0_DEBUG:
        dbg = {}
        dbg["d_qT"] = nc.dram_tensor("d_qT", [128, 2 * S], BF16, kind="ExternalOutput").ap()
        dbg["d_kT"] = nc.dram_tensor("d_kT", [128, 2 * S], BF16, kind="ExternalOutput").ap()
        dbg["d_vtok"] = nc.dram_tensor("d_vtok", [128, NTB * 256], BF16, kind="ExternalOutput").ap()
        dbg["d_qsh"] = nc.dram_tensor("d_qsh", [128, 2 * NTB], F32, kind="ExternalOutput").ap()
        dbg["d_p"] = nc.dram_tensor("d_p", [128, 512], BF16, kind="ExternalOutput").ap()
        dbg["d_sacc"] = nc.dram_tensor("d_sacc", [128, 20], F32, kind="ExternalOutput").ap()
        dbg["d_wvr"] = nc.dram_tensor("d_wvr", [128, NC * 256], BF16, kind="ExternalOutput").ap()
    xbatch = nc.dram_tensor("xbatch", [S, D], F32, kind="ExternalInput").ap()
    xloc = nc.dram_tensor("xloc", [T, D], F32, kind="ExternalInput").ap()
    wq = nc.dram_tensor("wq", [D, 256], F32, kind="ExternalInput")
    wk = nc.dram_tensor("wk", [D, 256], F32, kind="ExternalInput")
    wv = nc.dram_tensor("wv", [D, 2048], F32, kind="ExternalInput")
    bq = nc.dram_tensor("bq", [256], F32, kind="ExternalInput")
    bk = nc.dram_tensor("bk", [256], F32, kind="ExternalInput")
    bv = nc.dram_tensor("bv", [2048], F32, kind="ExternalInput")
    gamma = nc.dram_tensor("gamma", [D], F32, kind="ExternalInput")
    beta = nc.dram_tensor("beta", [D], F32, kind="ExternalInput")
    ident = nc.dram_tensor("ident", [128, 128], BF16, kind="ExternalInput").ap()
    identf = nc.dram_tensor("identf", [128, 128], F32, kind="ExternalInput").ap()
    onesr = nc.dram_tensor("onesr", [1, 128], BF16, kind="ExternalInput").ap()
    y = nc.dram_tensor("y", [T, D], F32, kind="ExternalOutput").ap()

    from contextlib import ExitStack
    with ExitStack() as ctx:
        e = ctx.enter_context
        qT = e(nc.sbuf_tensor("qT", [128, 2, S], BF16))
        kT = e(nc.sbuf_tensor("kT", [128, 2, S], BF16))
        vtok = e(nc.sbuf_tensor("vtok", [128, NTB, 264], BF16))
        xnT = [e(nc.sbuf_tensor(f"xnT{i}", [128, NC, 512], BF16)) for i in range(2)]
        wqs = e(nc.sbuf_tensor("wqs", [128, NC, 256], BF16))
        wks = e(nc.sbuf_tensor("wks", [128, NC, 256], BF16))
        wvr = e(nc.sbuf_tensor("wvr", [128, NC, 256], BF16))
        wvtmp = e(nc.sbuf_tensor("wvtmp", [128, 2048], F32))
        red1 = e(nc.sbuf_tensor("red1", [128, 1024], F32))
        red2 = e(nc.sbuf_tensor("red2", [128, 512], F32))
        grep = e(nc.sbuf_tensor("grep", [128, D], F32))
        brep = e(nc.sbuf_tensor("brep", [128, D], F32))
        bqc = e(nc.sbuf_tensor("bqc", [128, 2], F32))
        bkc = e(nc.sbuf_tensor("bkc", [128, 2], F32))
        bvt = e(nc.sbuf_tensor("bvt", [1, 2048], F32))
        bvr1 = e(nc.sbuf_tensor("bvr1", [1, 1024], F32))
        bvrow = e(nc.sbuf_tensor("bvrow", [1, 256], BF16))
        idt = e(nc.sbuf_tensor("idt", [128, 128], BF16))
        idtf = e(nc.sbuf_tensor("idtf", [128, 128], F32))
        ones = e(nc.sbuf_tensor("ones", [1, 128], BF16))
        epsc = e(nc.sbuf_tensor("epsc", [128, 1], F32))
        xt = [e(nc.sbuf_tensor(f"xt{i}", [128, D], F32)) for i in range(2)]
        sq = e(nc.sbuf_tensor("sq0", [128, D], F32))
        yln = [e(nc.sbuf_tensor(f"yln{i}", [128, D], BF16)) for i in range(2)]
        stat = [e(nc.sbuf_tensor(f"stat{i}", [128, 8], F32)) for i in range(2)]
        sqs = e(nc.sbuf_tensor("sqs", [128, 128], BF16))
        qn = e(nc.sbuf_tensor("qn", [128, NTB], F32))
        qsh = e(nc.sbuf_tensor("qsh", [128, 2, NTB], F32))
        kn = e(nc.sbuf_tensor("kn", [128, NTB], F32))
        knm = e(nc.sbuf_tensor("knm", [128, 4], F32))
        knmb = e(nc.sbuf_tensor("knmb", [128, 4], BF16))
        kmrow = e(nc.sbuf_tensor("kmrow", [64, 4], F32))
        kmcol = e(nc.sbuf_tensor("kmcol", [128, 1], F32))
        kmbf = e(nc.sbuf_tensor("kmbf", [1, 1], BF16))
        psb = [e(nc.sbuf_tensor(f"psb{i}", [128, 512], BF16)) for i in range(2)]
        ptb = [e(nc.sbuf_tensor(f"ptb{i}", [128, 512], BF16)) for i in range(2)]
        sacc = [e(nc.sbuf_tensor(f"sacc{i}", [128, 20], F32)) for i in range(2)]
        attf = [e(nc.sbuf_tensor(f"attf{i}", [128, 128], F32)) for i in range(2)]
        band = [e(nc.sbuf_tensor(f"band{i}", [128, D], F32)) for i in range(2)]
        ystage = [e(nc.sbuf_tensor(f"ystage{i}", [128, D], F32)) for i in range(2)]

        psA = e(nc.psum_tensor("psA", [128, 512], F32))
        psB = e(nc.psum_tensor("psB", [128, 512], F32))
        psC = e(nc.psum_tensor("psC", [128, 512], F32))
        psD = e(nc.psum_tensor("psD", [128, 512], F32))
        psE = e(nc.psum_tensor("psE", [128, 512], F32))
        psF = e(nc.psum_tensor("psF", [128, 512], F32))
        psG = e(nc.psum_tensor("psG", [128, 1024], BF16))
        psH = e(nc.psum_tensor("psH", [128, 1024], BF16))

        sc = Sched(nc)
        sc.psum_keys |= {"psA", "psB", "psC", "psD", "psE", "psF", "psG", "psH"}

        sc.op("vector", lambda v: v.memset(epsc[:, :], LN_EPS), writes=["epsc"])
        sc.dma("sync", idt[:, :], ident, writes=["idt"])
        sc.dma("sync", idtf[:, :], identf, writes=["idtf"])
        sc.dma("sync", ones[:, :], onesr, writes=["ones"])
        sc.dma("sync", grep[:, :], bass.AP(gamma, 0, [[0, 128], [1, D]]), writes=["grep"])
        sc.dma("sync", brep[:, :], bass.AP(beta, 0, [[0, 128], [1, D]]), writes=["brep"])
        sc.dma("sync", bqc[:, :], bass.AP(bq, 0, [[1, 128], [128, 2]]), writes=["bqc"],
               allow_slow_non_contiguous=True)
        sc.dma("sync", bkc[:, :], bass.AP(bk, 0, [[1, 128], [128, 2]]), writes=["bkc"],
               allow_slow_non_contiguous=True)
        for h in range(2):
            sc.op("vector", (lambda h: lambda v: v.memset(
                vtok[:, :, h * 132 + 128:h * 132 + 129], 1.0))(h),
                writes=[("vtok", it2) for it2 in range(NTB)])
        sc.dma("gpsimd", wqs[:, :, :], bass.AP(wq, 0, [[256, 128], [128 * 256, NC], [1, 256]]), writes=["wqs"])
        sc.dma("gpsimd", wks[:, :, :], bass.AP(wk, 0, [[256, 128], [128 * 256, NC], [1, 256]]), writes=["wks"])
        sc.dma("sync", bvt[:, :], bass.AP(bv, 0, [[2048, 1], [1, 2048]]), writes=["bvt"])
        sc.op("vector", lambda v: v.tensor_add(bvr1[:, :], bvt[:, 0:2048:2], bvt[:, 1:2048:2]),
              reads=["bvt"], writes=["bvr1"])
        sc.op("vector", lambda v: v.tensor_add(bvt[:, 0:512], bvr1[:, 0:1024:2], bvr1[:, 1:1024:2]),
              reads=["bvr1"], writes=["bvt"])
        sc.op("vector", lambda v: v.tensor_add(bvrow[:, :], bvt[:, 0:512:2], bvt[:, 1:512:2]),
              reads=["bvt"], writes=["bvrow"])
        for cc in range(NC):
            sc.dma("sync", wvtmp[:, :], bass.AP(wv, cc * 128 * 2048, [[2048, 128], [1, 2048]]),
                   writes=["wvtmp"])
            sc.op("vector", lambda v: v.tensor_add(red1[:, :], wvtmp[:, 0:2048:2], wvtmp[:, 1:2048:2]),
                  reads=["wvtmp"], writes=["red1"])
            sc.op("vector", lambda v: v.tensor_add(red2[:, :], red1[:, 0:1024:2], red1[:, 1:1024:2]),
                  reads=["red1"], writes=["red2"])
            sc.op("vector", (lambda cc: lambda v: v.tensor_add(
                wvr[:, cc, :], red2[:, 0:512:2], red2[:, 1:512:2]))(cc),
                reads=["red2"], writes=["wvr"])

        for sts in range(NSTS):
            xb = xnT[sts % 2]
            xk = f"xnT{sts % 2}"
            for j in range(4):
                it = sts * 4 + j
                bi = it % 2
                sc.dma("sync", xt[bi][:, :], xbatch[it * 128:(it + 1) * 128, :], writes=[f"xt{bi}"])
                emit_ln2_tile(sc, xt[bi][:, :], f"xt{bi}", yln[bi][:, :], f"yln{bi}",
                              stat[bi], f"stat{bi}", sq, "sq", epsc, grep, brep)
                for c in range(NC):
                    sc.op("tensor", (lambda bi, c: lambda t: t.transpose(
                        psG[:, c * 128:(c + 1) * 128], yln[bi][:, c * 128:(c + 1) * 128], idt[:, :]))(bi, c),
                        reads=[f"yln{bi}", "idt"], writes=["psG"])
                sc.op("vector", (lambda xb, j: lambda v: v.tensor_copy(
                    xb[:, :, j * 128:(j + 1) * 128],
                    psG[:, :].rearrange("p (c f) -> p c f", c=NC)))(xb, j),
                    reads=["psG"], writes=[xk])
            for h in range(2):
                pq, pk2, pv = (psA, psB, psC) if h == 0 else (psD, psE, psF)
                kq, kk, kv = (("psA", "psB", "psC") if h == 0 else ("psD", "psE", "psF"))
                for cc in range(NC):
                    sc.op("tensor", (lambda pq, h, cc, xb: lambda t: t.matmul(
                        pq[:, 0:512], wqs[:, cc, h * 128:(h + 1) * 128], xb[:, cc, :],
                        start=(cc == 0), stop=(cc == NC - 1)))(pq, h, cc, xb),
                        reads=["wqs", xk], writes=[kq])
                    sc.op("tensor", (lambda pk2, h, cc, xb: lambda t: t.matmul(
                        pk2[:, 0:512], wks[:, cc, h * 128:(h + 1) * 128], xb[:, cc, :],
                        start=(cc == 0), stop=(cc == NC - 1)))(pk2, h, cc, xb),
                        reads=["wks", xk], writes=[kk])
                sc.op("vector", (lambda pq, h, sts: lambda v: v.tensor_scalar(
                    qT[:, h, sts * 512:(sts + 1) * 512], pq[:, 0:512], bqc[:, h:h + 1], None, ALU.add))(pq, h, sts),
                    reads=[kq, "bqc"], writes=[("qT", h, sts)])
                sc.op("vector", (lambda pk2, h, sts: lambda v: v.tensor_scalar(
                    kT[:, h, sts * 512:(sts + 1) * 512], pk2[:, 0:512], bkc[:, h:h + 1], None, ALU.add))(pk2, h, sts),
                    reads=[kk, "bkc"], writes=[("kT", h, sts)])
            for j in range(4):
                it = sts * 4 + j
                pv = psC if j % 2 == 0 else psF
                kv = "psC" if j % 2 == 0 else "psF"
                for cc in range(NC):
                    sc.op("tensor", (lambda pv, cc, xb, j: lambda t: t.matmul(
                        pv[:, 0:256], xb[:, cc, j * 128:(j + 1) * 128], wvr[:, cc, :],
                        start=(cc == 0), stop=False))(pv, cc, xb, j),
                        reads=["wvr", xk], writes=[kv])
                sc.op("tensor", (lambda pv: lambda t: t.matmul(
                    pv[:, 0:256], ones[0:1, :], bvrow[0:1, :],
                    start=False, stop=True))(pv),
                    reads=["ones", "bvrow"], writes=[kv])
                sc.op("vector", (lambda pv, it: lambda v: v.tensor_copy(
                    vtok[:, it, 0:128], pv[:, 0:128]))(pv, it),
                    reads=[kv], writes=[("vtok", it)])
                sc.op("vector", (lambda pv, it: lambda v: v.tensor_copy(
                    vtok[:, it, 132:260], pv[:, 128:256]))(pv, it),
                    reads=[kv], writes=[("vtok", it)])

        if ATTN0_DEBUG:
            sc.dma("gpsimd", dbg["d_wvr"].rearrange("p (c g) -> p c g", c=NC), wvr[:, :, :],
                   reads=["wvr"])
            sc.dma("gpsimd", dbg["d_qT"].rearrange("p (h t) -> p h t", h=2), qT[:, :, :],
                   reads=[("qT", h2, s2) for h2 in range(2) for s2 in range(NSTS)])
            sc.dma("gpsimd", dbg["d_kT"].rearrange("p (h t) -> p h t", h=2), kT[:, :, :],
                   reads=[("kT", h2, s2) for h2 in range(2) for s2 in range(NSTS)])
            sc.dma("gpsimd", dbg["d_vtok"].rearrange("p (a t) -> p a t", a=NTB), vtok[:, :, :],
                   reads=[("vtok", it2) for it2 in range(NTB)])

        for h in range(2):
            for tb in range(NTB):
                pdst, kdst = (psG, "psG") if tb % 2 == 0 else (psH, "psH")
                sc.op("tensor", (lambda h, tb, pdst: lambda t: t.transpose(
                    pdst[:, 0:128], qT[:, h, tb * 128:(tb + 1) * 128], idt[:, :]))(h, tb, pdst),
                    reads=[("qT", h, tb // 4), "idt"], writes=[kdst])
                sc.op("scalar", (lambda tb, pdst: lambda s: s.activation(
                    sqs[:, :], pdst[:, 0:128], AF.Square))(tb, pdst),
                    reads=[kdst], writes=["sqs"])
                sc.op("vector", (lambda tb: lambda v: v.reduce_sum(
                    qn[:, tb:tb + 1], sqs[:, :], axis=AX.X))(tb),
                    reads=["sqs"], writes=["qn"])
                sc.op("tensor", (lambda h, tb, pdst: lambda t: t.transpose(
                    pdst[:, 128:256], kT[:, h, tb * 128:(tb + 1) * 128], idt[:, :]))(h, tb, pdst),
                    reads=[("kT", h, tb // 4), "idt"], writes=[kdst])
                sc.op("scalar", (lambda tb, pdst: lambda s: s.activation(
                    sqs[:, :], pdst[:, 128:256], AF.Square))(tb, pdst),
                    reads=[kdst], writes=["sqs"])
                sc.op("vector", (lambda tb: lambda v: v.reduce_sum(
                    kn[:, tb:tb + 1], sqs[:, :], axis=AX.X))(tb),
                    reads=["sqs"], writes=["kn"])
            sc.op("vector", lambda v: v.reduce_max(knm[:, 0:1], kn[:, :], axis=AX.X),
                  reads=["kn"], writes=["knm"])
            sc.op("vector", lambda v: v.tensor_copy(knmb[:, 0:1], knm[:, 0:1]),
                  reads=["knm"], writes=["knmb"])
            sc.op("tensor", lambda t: t.transpose(psG[0:1, 0:128], knmb[:, 0:1], idt[:, :]),
                  reads=["knmb", "idt"], writes=["psG"])
            sc.op("vector", lambda v: v.reduce_max(kmrow[0:1, 0:1], psG[0:1, 0:128], axis=AX.X),
                  reads=["psG"], writes=["kmrow"])
            sc.op("scalar", lambda s: s.activation(kmbf[0:1, 0:1], kmrow[0:1, 0:1], AF.Sqrt),
                  reads=["kmrow"], writes=["kmbf"])
            sc.op("tensor", lambda t: t.matmul(psF[:, 0:1], ones[0:1, :], kmbf[0:1, 0:1], start=True, stop=True),
                  reads=["ones", "kmbf"], writes=["psF"])
            sc.op("vector", lambda v: v.tensor_copy(kmcol[:, :], psF[:, 0:1]),
                  reads=["psF"], writes=["kmcol"])
            sc.op("scalar", lambda s: s.activation(qn[:, :], qn[:, :], AF.Sqrt),
                  reads=["qn"], writes=["qn"])
            sc.op("vector", lambda v: v.tensor_scalar(qn[:, :], qn[:, :], kmcol[:, 0:1], None, ALU.mult),
                  reads=["qn", "kmcol"], writes=["qn"])
            sc.op("vector", (lambda h: lambda v: v.tensor_scalar(
                qsh[:, h, :], qn[:, :], -ATT_SCALE, SHIFT_MARGIN, ALU.mult, ALU.add))(h),
                reads=["qn"], writes=[("qsh", h)])

        if ATTN0_DEBUG:
            sc.dma("gpsimd", dbg["d_qsh"].rearrange("p (h t) -> p h t", h=2), qsh[:, :, :],
                   reads=[("qsh", 0), ("qsh", 1)])

        for h in range(2):
            for bt in range(8):
                bb = band[bt % 2]
                bkey = f"band{bt % 2}"
                for jq in range(8):
                    qt = bt * 8 + jq
                    pscore, kscore = (psA, "psA") if qt % 2 == 0 else (psB, "psB")
                    patt, katt = (psC, "psC") if qt % 2 == 0 else (psD, "psD")
                    pb = psb[qt % 2]
                    ptb_ = ptb[qt % 2]
                    sa = sacc[qt % 2]
                    for kb in range(16):
                        sc.op("tensor", (lambda h, qt, kb, pscore: lambda t: t.matmul(
                            pscore[:, :], qT[:, h, qt * 128:(qt + 1) * 128],
                            kT[:, h, kb * 512:(kb + 1) * 512],
                            start=True, stop=True))(h, qt, kb, pscore),
                            reads=[("qT", h, qt // 4)] + [("kT", h, kb)], writes=[kscore])
                        sc.op("scalar", (lambda h, qt, kb, pscore, pb: lambda s: s.activation(
                            pb[:, :], pscore[:, :], AF.Exp, bias=qsh[:, h, qt:qt + 1],
                            scale=ATT_SCALE))(h, qt, kb, pscore, pb),
                            reads=[kscore, ("qsh", h)], writes=[f"psb{qt % 2}"])
                        if ATTN0_DEBUG and h == 0 and qt == 0 and kb == 3:
                            sc.dma("gpsimd", dbg["d_p"], pb[:, :], reads=[f"psb{qt % 2}"])
                        pt_ps, pt_key = (psG, "psG") if kb % 2 == 0 else (psH, "psH")
                        for jj in range(4):
                            sc.op("tensor", (lambda pb, jj, pt_ps: lambda t: t.transpose(
                                pt_ps[:, jj * 128:(jj + 1) * 128], pb[:, jj * 128:(jj + 1) * 128], idt[:, :]))(pb, jj, pt_ps),
                                reads=[f"psb{qt % 2}", "idt"], writes=[pt_key])
                        sc.op("vector", (lambda ptb_, pt_ps: lambda v: v.tensor_copy(
                            ptb_[:, :], pt_ps[:, 0:512]))(ptb_, pt_ps),
                            reads=[pt_key], writes=[f"ptb{qt % 2}"])
                        for jj in range(4):
                            it = kb * 4 + jj
                            sc.op("tensor", (lambda h, ptb_, jj, it, patt, kb: lambda t: t.matmul(
                                patt[:, 0:129], ptb_[:, jj * 128:(jj + 1) * 128],
                                vtok[:, it, h * 132:h * 132 + 129],
                                start=(kb == 0 and jj == 0), stop=(kb == 15 and jj == 3)))(h, ptb_, jj, it, patt, kb),
                                reads=[f"ptb{qt % 2}", ("vtok", it)], writes=[katt])
                    if ATTN0_DEBUG and h == 0 and qt == 0:
                        sc.dma("gpsimd", dbg["d_sacc"], sa[:, :], reads=[f"sacc{qt % 2}", f"sacc{qt % 2}t", f"sacc{qt % 2}f", f"sacc{qt % 2}r"])
                    sc.op("vector", (lambda sa, patt: lambda v: v.tensor_scalar(
                        sa[:, 17:18], patt[:, 128:129], SUM_FLOOR, None, ALU.add))(sa, patt),
                        reads=[katt], writes=[f"sacc{qt % 2}f"])
                    sc.op("vector", (lambda sa: lambda v: v.reciprocal(sa[:, 18:19], sa[:, 17:18]))(sa),
                          reads=[f"sacc{qt % 2}f"], writes=[f"sacc{qt % 2}r"])
                    af = attf[qt % 2]
                    sc.op("vector", (lambda h, patt, sa, af: lambda v: v.tensor_scalar(
                        af[:, :], patt[:, 0:128],
                        sa[:, 18:19], None, ALU.mult))(h, patt, sa, af),
                        reads=[katt, f"sacc{qt % 2}r"], writes=[f"attf{qt % 2}"])
                    sc.dma("gpsimd", bb[jq * 16:(jq + 1) * 16, :], af[:, :],
                           reads=[f"attf{qt % 2}"], writes=[(bkey, jq)])
                ib = h * 8 + bt
                bi = ib % 2
                sc.dma("sync", xt[bi][:, :], xloc[ib * 128:(ib + 1) * 128, :], writes=[f"xt{bi}"])
                sc.op("vector", (lambda bb, bi: lambda v: v.tensor_add(
                    ystage[bi][:, :], bb[:, :], xt[bi][:, :]))(bb, bi),
                    reads=[(bkey, jq2) for jq2 in range(8)] + [f"xt{bi}"], writes=[f"ystage{bi}"])
                sc.dma("scalar", y[ib * 128:(ib + 1) * 128, :], ystage[bi][:, :], reads=[f"ystage{bi}"])

        run_sched(nc, sc, ctx)
    return nc




TCH = 128
NCH = S // TCH
BETA_MIN, BETA_MAX = 0.85, 0.999


ATTN1_DEBUG = False

def build_attn1a():
    nc = bass.Bass("TRN2", target_bir_lowering=False)
    if ATTN1_DEBUG:
        dbg = {}
        for n, shp, dt in [("d_qtok", [128, NCH * 256], BF16), ("d_ktok", [128, NCH * 256], BF16),
                           ("d_vtok", [128, NCH * 256], BF16), ("d_ab", [128, NCH * 4], F32),
                           ("d_lnab", [128, NCH * 4], F32),
                           ("d_ngc", [128, 2 * NCH], F32), ("d_qsc", [128, 2 * NCH], F32),
                           ("d_ksc", [128, 2 * NCH], F32), ("d_ctc", [128, 2 * NCH], F32),
                           ("d_smb", [128, 128], BF16)]:
            dbg[n] = nc.dram_tensor(n, shp, dt, kind="ExternalOutput").ap()
    xbatch = nc.dram_tensor("xbatch", [S, D], F32, kind="ExternalInput").ap()
    wqt = nc.dram_tensor("wqt", [D, 256], F32, kind="ExternalInput")
    wkt = nc.dram_tensor("wkt", [D, 256], F32, kind="ExternalInput")
    wvt = nc.dram_tensor("wvt", [D, 256], F32, kind="ExternalInput")
    wab = nc.dram_tensor("wab", [D, 4], F32, kind="ExternalInput")
    bab = nc.dram_tensor("bab", [1, 4], F32, kind="ExternalInput")
    ctab = nc.dram_tensor("ctab", [128, NCH, 128], BF16, kind="ExternalInput").ap()
    stab = nc.dram_tensor("stab", [128, NCH, 128], BF16, kind="ExternalInput").ap()
    lneg = nc.dram_tensor("lneg", [128, 128], BF16, kind="ExternalInput").ap()
    umask = nc.dram_tensor("umask", [128, 128], BF16, kind="ExternalInput").ap()
    ident = nc.dram_tensor("ident", [128, 128], BF16, kind="ExternalInput").ap()
    identf = nc.dram_tensor("identf", [128, 128], F32, kind="ExternalInput").ap()
    onesr = nc.dram_tensor("onesr", [1, 128], BF16, kind="ExternalInput").ap()
    o5 = nc.dram_tensor("o5", [S, 256], BF16, kind="ExternalOutput").ap()

    from contextlib import ExitStack
    with ExitStack() as ctx:
        e = ctx.enter_context
        qtok = e(nc.sbuf_tensor("qtok", [128, NCH, 256], BF16))
        ktok = e(nc.sbuf_tensor("ktok", [128, NCH, 256], BF16))
        vtok = e(nc.sbuf_tensor("vtok", [128, NCH, 256], BF16))
        rsc = e(nc.sbuf_tensor("rsc", [128, NCH, 128], BF16))
        ctb = e(nc.sbuf_tensor("ctb", [128, NCH, 128], BF16))
        stb = e(nc.sbuf_tensor("stb", [128, NCH, 128], BF16))
        wq_s = e(nc.sbuf_tensor("wq_s", [128, NC, 256], BF16))
        wk_s = e(nc.sbuf_tensor("wk_s", [128, NC, 256], BF16))
        wv_s = e(nc.sbuf_tensor("wv_s", [128, NC, 256], BF16))
        wab_s = e(nc.sbuf_tensor("wab_s", [128, NC, 4], BF16))
        babr = e(nc.sbuf_tensor("babr", [1, 4], BF16))
        lnegs = e(nc.sbuf_tensor("lnegs", [128, 128], BF16))
        lnhi = e(nc.sbuf_tensor("lnhi", [128, NCH, 4], BF16))
        lnlo = e(nc.sbuf_tensor("lnlo", [128, NCH, 4], BF16))
        lnres = e(nc.sbuf_tensor("lnres", [128, NCH, 4], F32))
        onescb = e(nc.sbuf_tensor("onescb", [128, 1], BF16))
        stmpb = e(nc.sbuf_tensor("stmpb", [128, 1], BF16))
        um = e(nc.sbuf_tensor("um", [128, 128], BF16))
        idt = e(nc.sbuf_tensor("idt", [128, 128], BF16))
        idtf = e(nc.sbuf_tensor("idtf", [128, 128], F32))
        ones = e(nc.sbuf_tensor("ones", [1, 128], BF16))
        onescolf = e(nc.sbuf_tensor("onescolf", [128, 1], F32))
        epsc = e(nc.sbuf_tensor("epsc", [128, 1], F32))
        xt = [e(nc.sbuf_tensor(f"xt{i}", [128, D], F32)) for i in range(2)]
        sq = e(nc.sbuf_tensor("sq0", [128, D], F32))
        yln = [e(nc.sbuf_tensor(f"yln{i}", [128, D], BF16)) for i in range(2)]
        stat = [e(nc.sbuf_tensor(f"stat{i}", [128, 8], F32)) for i in range(2)]
        xnt = [e(nc.sbuf_tensor(f"xnt{i}", [128, NC, 128], BF16)) for i in range(2)]
        abtok = e(nc.sbuf_tensor("abtok", [128, NCH, 4], F32))
        lnab = e(nc.sbuf_tensor("lnab", [128, NCH, 4], F32))
        ngc = e(nc.sbuf_tensor("ngc", [128, 2, NCH], F32))
        kscratch = e(nc.sbuf_tensor("kscratch", [128, NCH], F32))
        qsc = e(nc.sbuf_tensor("qsc", [128, 2, NCH], F32))
        ksc = e(nc.sbuf_tensor("ksc", [128, 2, NCH], F32))
        ctc = e(nc.sbuf_tensor("ctc", [128, 2, NCH], F32))
        ctr = e(nc.sbuf_tensor("ctr", [1, 2, NCH], BF16))
        ctrf = e(nc.sbuf_tensor("ctrf", [1, 2, NCH], F32))
        s0 = [e(nc.sbuf_tensor(f"s0_{h}", [128, 128], BF16)) for h in range(2)]
        stmp = e(nc.sbuf_tensor("stmp", [128, 128], F32))
        qsb = e(nc.sbuf_tensor("qsb", [128, 128], BF16))
        ksb = e(nc.sbuf_tensor("ksb", [128, 128], BF16))
        qft = e(nc.sbuf_tensor("qft", [128, 128], BF16))
        kft = e(nc.sbuf_tensor("kft", [128, 128], BF16))
        smb = e(nc.sbuf_tensor("smb", [128, 128], BF16))
        ob = [e(nc.sbuf_tensor(f"ob{i}", [128, 128], BF16)) for i in range(2)]

        pQKV = e(nc.psum_tensor("pQKV", [128, 772], F32))
        pTr = e(nc.psum_tensor("pTr", [128, 1024], BF16))
        pS = e(nc.psum_tensor("pS", [128, 128], F32))
        pO = [e(nc.psum_tensor(f"pO{i}", [128, 128], F32)) for i in range(2)]
        pU = e(nc.psum_tensor("pU", [128, 128], F32))

        sc = Sched(nc)
        sc.psum_keys |= {"pQKV", "pTr", "pS", "pO0", "pO1", "pU"}

        sc.op("vector", lambda v: v.memset(epsc[:, :], LN_EPS), writes=["epsc"])
        sc.op("vector", lambda v: v.memset(onescolf[:, :], 1.0), writes=["onescolf"])
        sc.op("vector", lambda v: v.memset(onescb[:, :], 1.0), writes=["onescb"])
        for h in range(2):
            sc.op("vector", (lambda h: lambda v: v.memset(s0[h][:, :], 0.0))(h), writes=[f"s0_{h}"])
        sc.dma("sync", idt[:, :], ident, writes=["idt"])
        sc.dma("sync", idtf[:, :], identf, writes=["idtf"])
        sc.dma("sync", ones[:, :], onesr, writes=["ones"])
        sc.dma("sync", lnegs[:, :], lneg, writes=["lnegs"])
        sc.dma("sync", um[:, :], umask, writes=["um"])
        sc.dma("sync", ctb[:, :, :], ctab, writes=["ctb"])
        sc.dma("sync", stb[:, :, :], stab, writes=["stb"])
        sc.dma("gpsimd", babr[:, :], bass.AP(bab, 0, [[4, 1], [1, 4]]), writes=["babr"])
        for wt, buf, k in ((wqt, wq_s, "wq_s"), (wkt, wk_s, "wk_s"), (wvt, wv_s, "wv_s")):
            sc.dma("gpsimd", buf[:, :, :],
                   bass.AP(wt, 0, [[256, 128], [128 * 256, NC], [1, 256]]), writes=[k])
        sc.dma("gpsimd", wab_s[:, :, :], bass.AP(wab, 0, [[4, 128], [128 * 4, NC], [1, 4]]), writes=["wab_s"])

        for it in range(NCH):
            bi = it % 2
            xb = xnt[bi]
            xk = f"xnt{bi}"
            sc.dma("sync", xt[bi][:, :], xbatch[it * 128:(it + 1) * 128, :], writes=[f"xt{bi}"])
            emit_ln_tile(sc, xt[bi][:, :], f"xt{bi}", yln[bi][:, :], f"yln{bi}",
                         stat[bi], f"stat{bi}", sq, "sq", epsc=epsc)
            for c in range(NC):
                sc.op("tensor", (lambda bi, c: lambda t: t.transpose(
                    pTr[:, c * 128:(c + 1) * 128], yln[bi][:, c * 128:(c + 1) * 128], idt[:, :]))(bi, c),
                    reads=[f"yln{bi}", "idt"], writes=["pTr"])
            sc.op("vector", (lambda xb: lambda v: v.tensor_copy(
                xb[:, :, :], pTr[:, :].rearrange("p (c f) -> p c f", c=NC)))(xb),
                reads=["pTr"], writes=[xk])
            for cc in range(NC):
                sc.op("tensor", (lambda xb, cc: lambda t: t.matmul(
                    pQKV[:, 0:256], xb[:, cc, :], wq_s[:, cc, :],
                    start=(cc == 0), stop=(cc == NC - 1)))(xb, cc),
                    reads=[xk, "wq_s"], writes=["pQKV"])
                sc.op("tensor", (lambda xb, cc: lambda t: t.matmul(
                    pQKV[:, 256:512], xb[:, cc, :], wk_s[:, cc, :],
                    start=False, stop=(cc == NC - 1)))(xb, cc),
                    reads=[xk, "wk_s"], writes=["pQKV"])
                sc.op("tensor", (lambda xb, cc: lambda t: t.matmul(
                    pQKV[:, 512:768], xb[:, cc, :], wv_s[:, cc, :],
                    start=(cc == 0), stop=(cc == NC - 1)))(xb, cc),
                    reads=[xk, "wv_s"], writes=["pQKV"])
                sc.op("tensor", (lambda xb, cc: lambda t: t.matmul(
                    pQKV[:, 768:772], xb[:, cc, :], wab_s[:, cc, :],
                    start=False, stop=False))(xb, cc),
                    reads=[xk, "wab_s"], writes=["pQKV"])
            sc.op("tensor", lambda t: t.matmul(
                pQKV[:, 768:772], ones[0:1, :], babr[0:1, :], start=False, stop=True),
                reads=["ones", "babr"], writes=["pQKV"])
            sc.op("vector", (lambda it: lambda v: v.tensor_copy(qtok[:, it, :], pQKV[:, 0:256]))(it),
                  reads=["pQKV"], writes=[("qtok", it)])
            sc.op("vector", (lambda it: lambda v: v.tensor_copy(ktok[:, it, :], pQKV[:, 256:512]))(it),
                  reads=["pQKV"], writes=[("ktok", it)])
            sc.op("vector", (lambda it: lambda v: v.tensor_copy(vtok[:, it, :], pQKV[:, 512:768]))(it),
                  reads=["pQKV"], writes=[("vtok", it)])
            sc.op("vector", (lambda it: lambda v: v.tensor_copy(abtok[:, it, :], pQKV[:, 768:772]))(it),
                  reads=["pQKV"], writes=[("abtok", it)])

        ALLQ = [("qtok", i) for i in range(NCH)]
        ALLK = [("ktok", i) for i in range(NCH)]
        ALLV = [("vtok", i) for i in range(NCH)]
        ALLAB = [("abtok", i) for i in range(NCH)]

        for tn, tk, allk in ((qtok, "qtok", ALLQ), (ktok, "ktok", ALLK)):
            for h in range(2):
                sc.op("vector", (lambda tn, h: lambda v: v.tensor_scalar(
                    rsc[:, :, 0:128:2], tn[:, :, h * 128 + 1:h * 128 + 128:2], -1.0, None, ALU.mult))(tn, h),
                    reads=allk, writes=["rsc"])
                sc.op("vector", (lambda tn, h: lambda v: v.tensor_copy(
                    rsc[:, :, 1:128:2], tn[:, :, h * 128:h * 128 + 128:2]))(tn, h),
                    reads=allk, writes=["rsc"])
                sc.op("gpsimd", lambda g: g.tensor_mul(rsc[:, :, :], rsc[:, :, :], stb[:, :, :]),
                      reads=["rsc", "stb"], writes=["rsc"])
                sc.op("gpsimd", (lambda tn, h: lambda g: g.tensor_mul(
                    tn[:, :, h * 128:(h + 1) * 128], tn[:, :, h * 128:(h + 1) * 128], ctb[:, :, :]))(tn, h),
                    reads=allk + ["ctb"], writes=allk)
                sc.op("vector", (lambda tn, h: lambda v: v.tensor_add(
                    tn[:, :, h * 128:(h + 1) * 128], tn[:, :, h * 128:(h + 1) * 128], rsc[:, :, :]))(tn, h),
                    reads=allk + ["rsc"], writes=allk)

        sc.op("scalar", lambda s: s.activation(abtok[:, :, :], abtok[:, :, :], AF.Sigmoid),
              reads=ALLAB, writes=["ab2"] + ALLAB)
        sc.op("vector", lambda v: v.tensor_scalar(abtok[:, :, 2:4], abtok[:, :, 2:4],
                                                  BETA_MAX - BETA_MIN, BETA_MIN, ALU.mult, ALU.add),
              reads=["ab2"], writes=["ab3"])
        sc.op("scalar", lambda s: s.activation(lnab[:, :, :], abtok[:, :, :], AF.Ln),
              reads=["ab2", "ab3"], writes=["lnab"])
        sc.op("vector", lambda v: v.tensor_copy(lnhi[:, :, :], lnab[:, :, :]),
              reads=["lnab"], writes=["lnhi"])
        sc.op("vector", lambda v: v.tensor_sub(lnres[:, :, :], lnab[:, :, :], lnhi[:, :, :]),
              reads=["lnab", "lnhi"], writes=["lnres"])
        sc.op("vector", lambda v: v.tensor_copy(lnlo[:, :, :], lnres[:, :, :]),
              reads=["lnres"], writes=["lnlo"])
        for h in range(2):
            for j in range(NCH):
                sc.op("tensor", (lambda h, j: lambda t: t.matmul(
                    pS[:, j:j + 1], lnegs[:, :], lnhi[:, j, 2 + h:3 + h],
                    start=True, stop=False))(h, j),
                    reads=["lnegs", "lnhi"], writes=["pS"])
                sc.op("tensor", (lambda h, j: lambda t: t.matmul(
                    pS[:, j:j + 1], lnegs[:, :], lnlo[:, j, 2 + h:3 + h],
                    start=False, stop=True))(h, j),
                    reads=["lnegs", "lnlo"], writes=["pS"])
            sc.op("vector", (lambda h: lambda v: v.tensor_copy(ngc[:, h, :], pS[:, 0:NCH]))(h),
                  reads=["pS"], writes=[("ngc", h)])
            sc.op("scalar", (lambda h: lambda s: s.activation(
                qsc[:, h, :], ngc[:, h, :], AF.Exp, scale=-1.0))(h),
                reads=[("ngc", h)], writes=[("qsc", h)])
            sc.op("vector", (lambda h: lambda v: v.tensor_add(
                kscratch[:, :], ngc[:, h, :], lnab[:, :, h]))(h),
                reads=[("ngc", h), "lnab"], writes=["kscratch"])
            sc.op("scalar", (lambda h: lambda s: s.activation(
                ksc[:, h, :], kscratch[:, :], AF.Exp))(h),
                reads=["kscratch"], writes=[("ksc", h)])
            sc.op("tensor", (lambda h: lambda t: t.matmul(
                pU[0:NCH, 0:1], lnhi[:, :, 2 + h], onescb[:, :], start=True, stop=False))(h),
                reads=["lnhi", "onescb"], writes=["pU"])
            sc.op("tensor", (lambda h: lambda t: t.matmul(
                pU[0:NCH, 0:1], lnlo[:, :, 2 + h], onescb[:, :], start=False, stop=True))(h),
                reads=["lnlo", "onescb"], writes=["pU"])
            sc.op("vector", lambda v: v.tensor_copy(stmpb[0:NCH, 0:1], pU[0:NCH, 0:1]),
                  reads=["pU"], writes=["stmpb"])
            sc.op("tensor", lambda t: t.transpose(pTr[0:1, 0:NCH], stmpb[0:NCH, 0:1], idt[0:NCH, 0:NCH]),
                  reads=["stmpb", "idt"], writes=["pTr"])
            sc.op("scalar", (lambda h: lambda s: s.activation(
                ctrf[0:1, h, :], pTr[0:1, 0:NCH], AF.Exp))(h),
                reads=["pTr"], writes=[("ctrf", h)])
            sc.op("vector", (lambda h: lambda v: v.tensor_copy(ctr[0:1, h, :], ctrf[0:1, h, :]))(h),
                  reads=[("ctrf", h)], writes=[("ctr", h)])
            for j in range(NCH):
                sc.op("tensor", (lambda h, j: lambda t: t.matmul(
                    pS[:, j:j + 1], ones[0:1, :], ctr[0:1, h, j:j + 1],
                    start=True, stop=True))(h, j),
                    reads=["ones", ("ctr", h)], writes=["pS"])
            sc.op("vector", (lambda h: lambda v: v.tensor_copy(ctc[:, h, :], pS[:, 0:NCH]))(h),
                  reads=["pS"], writes=[("ctc", h)])

        if ATTN1_DEBUG:
            sc.dma("gpsimd", dbg["d_qtok"].rearrange("p (a b) -> p a b", a=NCH), qtok[:, :, :], reads=ALLQ)
            sc.dma("gpsimd", dbg["d_ktok"].rearrange("p (a b) -> p a b", a=NCH), ktok[:, :, :], reads=ALLK)
            sc.dma("gpsimd", dbg["d_vtok"].rearrange("p (a b) -> p a b", a=NCH), vtok[:, :, :], reads=ALLV)
            sc.dma("gpsimd", dbg["d_ab"].rearrange("p (a b) -> p a b", a=NCH), abtok[:, :, :], reads=["ab2", "ab3"])
            sc.dma("gpsimd", dbg["d_lnab"].rearrange("p (a b) -> p a b", a=NCH), lnab[:, :, :], reads=["lnab"])
            sc.dma("gpsimd", dbg["d_ngc"].rearrange("p (a b) -> p a b", a=2), ngc[:, :, :], reads=[("ngc", 0), ("ngc", 1)])
            sc.dma("gpsimd", dbg["d_qsc"].rearrange("p (a b) -> p a b", a=2), qsc[:, :, :], reads=[("qsc", 0), ("qsc", 1)])
            sc.dma("gpsimd", dbg["d_ksc"].rearrange("p (a b) -> p a b", a=2), ksc[:, :, :], reads=[("ksc", 0), ("ksc", 1)])
            sc.dma("gpsimd", dbg["d_ctc"].rearrange("p (a b) -> p a b", a=2), ctc[:, :, :], reads=[("ctc", 0), ("ctc", 1)])

        for j in range(NCH):
            for h in range(2):
                hs = slice(h * 128, (h + 1) * 128)
                sc.op("vector", (lambda j, h: lambda v: v.tensor_scalar(
                    qsb[:, :], qtok[:, j, h * 128:(h + 1) * 128], qsc[:, h, j:j + 1], None, ALU.mult))(j, h),
                    reads=[("qtok", j), ("qsc", h)], writes=["qsb"])
                sc.op("vector", (lambda j, h: lambda v: v.tensor_scalar(
                    ksb[:, :], ktok[:, j, h * 128:(h + 1) * 128], ksc[:, h, j:j + 1], None, ALU.mult))(j, h),
                    reads=[("ktok", j), ("ksc", h)], writes=["ksb"])
                sc.op("tensor", lambda t: t.transpose(pTr[:, 0:128], qsb[:, :], idt[:, :]),
                      reads=["qsb", "idt"], writes=["pTr"])
                sc.op("tensor", lambda t: t.transpose(pTr[:, 128:256], ksb[:, :], idt[:, :]),
                      reads=["ksb", "idt"], writes=["pTr"])
                sc.op("vector", lambda v: v.tensor_copy(qft[:, :], pTr[:, 0:128]),
                      reads=["pTr"], writes=["qft"])
                sc.op("vector", lambda v: v.tensor_copy(kft[:, :], pTr[:, 128:256]),
                      reads=["pTr"], writes=["kft"])
                sc.op("tensor", lambda t: t.matmul(pS[:, :], kft[:, :], qft[:, :], start=True, stop=True),
                      reads=["kft", "qft"], writes=["pS"])
                sc.op("vector", lambda v: v.tensor_tensor(smb[:, :], pS[:, :], um[:, :], ALU.mult),
                      reads=["pS", "um"], writes=["smb"])
                if ATTN1_DEBUG and j == 0 and h == 0:
                    sc.dma("gpsimd", dbg["d_smb"], smb[:, :], reads=["smb"])
                po = pO[h]
                pok = f"pO{h}"
                sc.op("tensor", (lambda j, h, po: lambda t: t.matmul(
                    po[:, :], smb[:, :], vtok[:, j, h * 128:(h + 1) * 128], start=True, stop=False))(j, h, po),
                    reads=["smb", ("vtok", j)], writes=[pok])
                sc.op("tensor", (lambda h, po: lambda t: t.matmul(
                    po[:, :], qft[:, :], s0[h][:, :], start=False, stop=True))(h, po),
                    reads=["qft", f"s0_{h}"], writes=[pok])
                sc.op("tensor", (lambda j, h: lambda t: t.matmul(
                    pU[:, :], ksb[:, :], vtok[:, j, h * 128:(h + 1) * 128], start=True, stop=True))(j, h),
                    reads=["ksb", ("vtok", j)], writes=["pU"])
                sc.op("vector", (lambda h: lambda v: v.tensor_add(stmp[:, :], pU[:, :], s0[h][:, :]))(h),
                      reads=["pU", f"s0_{h}"], writes=["stmp"])
                sc.op("vector", (lambda j, h: lambda v: v.tensor_scalar(
                    s0[h][:, :], stmp[:, :], ctc[:, h, j:j + 1], None, ALU.mult))(j, h),
                    reads=["stmp", ("ctc", h)], writes=[f"s0_{h}"])
                obuf = ob[h]
                sc.op("scalar", (lambda po, obuf: lambda s: s.copy(obuf[:, :], po[:, :]))(po, obuf),
                      reads=[pok], writes=[f"ob{h}"])
                sc.dma("scalar", o5[j * 128:(j + 1) * 128, h * 128:(h + 1) * 128], obuf[:, :],
                       reads=[f"ob{h}"])

        run_sched(nc, sc, ctx)
    return nc


def _rope_tables():
    half = 64
    inv_freq = 1.0 / (10000.0 ** (np.arange(half, dtype=np.float64) / half))
    t = np.arange(S, dtype=np.float64)[:, None] * inv_freq[None, :]
    cos = np.cos(t); sin = np.sin(t)
    cexp = np.repeat(cos, 2, axis=1)
    sexp = np.repeat(sin, 2, axis=1)
    c3 = cexp.reshape(NCH, 128, 128).transpose(1, 0, 2)
    s3 = sexp.reshape(NCH, 128, 128).transpose(1, 0, 2)
    return np.ascontiguousarray(c3.astype(BF)), np.ascontiguousarray(s3.astype(BF))


def launch_attn1a(x2d, w):
    nc = _cached("attn1a", build_attn1a)
    ctab, stab = _cached("rope", _rope_tables)
    lneg = _cached("lneg", lambda: np.ascontiguousarray((-np.tril(np.ones((128, 128))).T).astype(BF)))
    umask = _cached("umask", lambda: np.triu(np.ones((128, 128))).astype(BF))
    in_maps = []
    for c in range(NCORE):
        b = c // 4
        hq = 2 * (c % 4)
        wabm = np.stack([w["Walpha"][hq], w["Walpha"][hq + 1],
                         w["Wbeta"][hq], w["Wbeta"][hq + 1]], axis=1)
        babm = np.array([[w["balpha"][hq], w["balpha"][hq + 1],
                          w["bbeta"][hq], w["bbeta"][hq + 1]]], np.float32)
        in_maps.append({
            "xbatch": np.ascontiguousarray(x2d[b * S:(b + 1) * S]),
            "wqt": np.ascontiguousarray(w["Wq"][hq * 128:(hq + 2) * 128, :].T),
            "wkt": np.ascontiguousarray(w["Wk"][hq * 128:(hq + 2) * 128, :].T),
            "wvt": np.ascontiguousarray(w["Wv"][hq * 128:(hq + 2) * 128, :].T),
            "wab": np.ascontiguousarray(wabm), "bab": babm,
            "ctab": ctab, "stab": stab, "lneg": lneg, "umask": umask,
            "ident": _cached("ident", _ident128),
            "identf": _cached("identf", lambda: np.eye(128, dtype=np.float32)),
            "onesr": _cached("onesr", lambda: np.ones((1, 128), dtype=BF)),
        })
    res = run_bass_kernel_spmd(nc, in_maps, core_ids=list(range(NCORE)))
    return [r["o5"] for r in res.results]


def build_attn1b():
    nc = bass.Bass("TRN2", target_bir_lowering=False)
    xloc = nc.dram_tensor("xloc", [T, D], F32, kind="ExternalInput").ap()
    otT = nc.dram_tensor("otT", [D, T], BF16, kind="ExternalInput")
    wot = nc.dram_tensor("wot", [D, D], F32, kind="ExternalInput")
    bo = nc.dram_tensor("bo", [D], F32, kind="ExternalInput")
    onesr = nc.dram_tensor("onesr", [1, 128], BF16, kind="ExternalInput").ap()
    y = nc.dram_tensor("y", [T, D], F32, kind="ExternalOutput").ap()
    from contextlib import ExitStack
    with ExitStack() as ctx:
        e = ctx.enter_context
        ot_s = e(nc.sbuf_tensor("ot_s", [128, NC, T], BF16))
        wo_s = e(nc.sbuf_tensor("wo_s", [128, NC, D], BF16))
        borow = e(nc.sbuf_tensor("borow", [1, D], BF16))
        ones = e(nc.sbuf_tensor("ones", [1, 128], BF16))
        xt = [e(nc.sbuf_tensor(f"xt{i}", [128, D], F32)) for i in range(2)]
        ystage = [e(nc.sbuf_tensor(f"ystage{i}", [128, D], F32)) for i in range(2)]
        pY = [e(nc.psum_tensor(f"pY{i}", [128, 1024], F32)) for i in range(2)]
        sc = Sched(nc)
        sc.psum_keys |= {"pY0", "pY1"}
        sc.dma("sync", ones[:, :], onesr, writes=["ones"])
        sc.dma("gpsimd", borow[:, :], bass.AP(bo, 0, [[D, 1], [1, D]]), writes=["borow"])
        sc.dma("gpsimd", ot_s[:, :, :], bass.AP(otT, 0, [[T, 128], [128 * T, NC], [1, T]]), writes=["ot_s"])
        sc.dma("gpsimd", wo_s[:, :, :], bass.AP(wot, 0, [[D, 128], [128 * D, NC], [1, D]]), writes=["wo_s"])
        for i in range(NT):
            bi = i % 2
            py = pY[bi]
            pyk = f"pY{bi}"
            sc.dma("sync", xt[bi][:, :], xloc[i * 128:(i + 1) * 128, :], writes=[f"xt{bi}"])
            for half in range(2):
                for cc in range(NC):
                    sc.op("tensor", (lambda i, cc, half, py: lambda t: t.matmul(
                        py[:, half * 512:(half + 1) * 512], ot_s[:, cc, i * 128:(i + 1) * 128],
                        wo_s[:, cc, half * 512:(half + 1) * 512],
                        start=(cc == 0), stop=False))(i, cc, half, py),
                        reads=["ot_s", "wo_s"], writes=[pyk])
                sc.op("tensor", (lambda half, py: lambda t: t.matmul(
                    py[:, half * 512:(half + 1) * 512], ones[0:1, :],
                    borow[0:1, half * 512:(half + 1) * 512],
                    start=False, stop=True))(half, py),
                    reads=["ones", "borow"], writes=[pyk])
            sc.op("vector", (lambda bi, py: lambda v: v.tensor_add(
                ystage[bi][:, :], py[:, :], xt[bi][:, :]))(bi, py),
                reads=[pyk, f"xt{bi}"], writes=[f"ystage{bi}"])
            sc.dma("scalar", y[i * 128:(i + 1) * 128, :], ystage[bi][:, :], reads=[f"ystage{bi}"])
        run_sched(nc, sc, ctx)
    return nc


def launch_attn1(x2d, w):
    olist = launch_attn1a(x2d, w)
    nc = _cached("attn1b", build_attn1b)
    wot = np.ascontiguousarray(w["Wo"].T)
    in_maps = []
    for c in range(NCORE):
        b = c // 4
        r0 = (c % 4) * T
        cols = [olist[b * 4 + hh // 2][r0:r0 + T, (hh % 2) * 128:(hh % 2 + 1) * 128] for hh in range(8)]
        ofull = np.concatenate(cols, axis=1)
        in_maps.append({
            "xloc": np.ascontiguousarray(x2d[c * T:(c + 1) * T]),
            "otT": np.ascontiguousarray(ofull.T),
            "wot": wot, "bo": w["bo"],
            "onesr": _cached("onesr", lambda: np.ones((1, 128), dtype=BF)),
        })
    res = run_bass_kernel_spmd(nc, in_maps, core_ids=list(range(NCORE)))
    return np.concatenate([r["y"] for r in res.results], axis=0)


def kernel(**inputs):
    g = {k: np.asarray(v) for k, v in inputs.items()}
    x = np.ascontiguousarray(g["L0_attn_x"].astype(np.float32, copy=False)).reshape(B * S, D)
    l0ffn = {k[len("L0_ffn_"):]: g[k] for k in g if k.startswith("L0_ffn_")}
    l0attn = {k[len("L0_attn_"):]: g[k] for k in g if k.startswith("L0_attn_") and k != "L0_attn_x"}
    l1ffn = {k[len("L1_ffn_"):]: g[k] for k in g if k.startswith("L1_ffn_")}
    l1attn = {k[len("L1_attn_"):]: g[k] for k in g if k.startswith("L1_attn_")}
    x = launch_ffn0(x, l0ffn)
    x = launch_attn0(x, l0attn)
    x = launch_ffn0(x, l0ffn)
    x = launch_moe(x, l1ffn)
    x = launch_attn1(x, l1attn)
    x = launch_moe(x, l1ffn)
    return x.reshape(B, S, D)
